# version 06
# baseline (speedup 1.0000x reference)
import numpy as np
from contextlib import ExitStack

import concourse.bass as bass
import concourse.mybir as mybir
from concourse.bass_utils import run_bass_kernel_spmd
from concourse.library_overlay import lower_extended_insts
from concourse import library_config

F32 = mybir.dt.float32
BF16 = mybir.dt.bfloat16
I16 = mybir.dt.int16
U32 = mybir.dt.uint32
AF = mybir.ActivationFunctionType
ALU = mybir.AluOpType
AX = mybir.AxisListType

NCORES = 8
T = 2048
SUB = 512
NSUB = T // SUB
NTT = SUB // 128
D = 1024
DC = D // 128
E = 32
CC = 384
XS_E = E * CC
ROWS = XS_E + T
H = 512
HC = H // 128
HS = 1024
EPS = 1e-5
C_EXP = 8.0


class Ctr:
    def __init__(self, h):
        self.h = h
        self.v = 0

    def inc(self, n=1):
        self.v += n
        return self.v


def build(debug=False, stage=6):
    nc = bass.Bass("TRN2", target_bir_lowering=False, detect_race_conditions=False)

    xin = nc.dram_tensor("xin", [T, D], F32, kind="ExternalInput").ap()
    wrt = nc.dram_tensor("wrt", [D, D], F32, kind="ExternalInput").ap()
    wit = nc.dram_tensor("wit", [D, D], F32, kind="ExternalInput").ap()
    lam = nc.dram_tensor("lam", [128, DC], F32, kind="ExternalInput").ap()
    wg = nc.dram_tensor("wg", [D, E], F32, kind="ExternalInput").ap()
    w1 = nc.dram_tensor("w1", [E, D, H], F32, kind="ExternalInput").ap()
    w2 = nc.dram_tensor("w2", [E, H, D], F32, kind="ExternalInput").ap()
    w3 = nc.dram_tensor("w3", [E, D, H], F32, kind="ExternalInput").ap()
    ws1 = nc.dram_tensor("ws1", [D, HS], F32, kind="ExternalInput").ap()
    ws2 = nc.dram_tensor("ws2", [HS, D], F32, kind="ExternalInput").ap()
    ws3 = nc.dram_tensor("ws3", [D, HS], F32, kind="ExternalInput").ap()
    cident = nc.dram_tensor("cident", [128, 128], F32, kind="ExternalInput").ap()
    clfullt = nc.dram_tensor("clfullt", [128, 128], F32, kind="ExternalInput").ap()
    cones = nc.dram_tensor("cones", [128, 128], F32, kind="ExternalInput").ap()
    crepl = nc.dram_tensor("crepl", [8, 128, 128], F32, kind="ExternalInput").ap()
    ciota = nc.dram_tensor("ciota", [128, E], F32, kind="ExternalInput").ap()
    cmask = nc.dram_tensor("cmask", [128, 4], F32, kind="ExternalInput").ap()

    out = nc.dram_tensor("out", [T, D], F32, kind="ExternalOutput").ap()

    sk = "ExternalOutput" if debug else "Internal"
    hbm_h = nc.dram_tensor("hbm_h", [D, T], F32, kind=sk)
    hbm_p = nc.dram_tensor("hbm_p", [D, T], BF16, kind=sk)
    xsorted = nc.dram_tensor("xsorted", [ROWS, D], BF16, kind=sk)
    ysorted = nc.dram_tensor("ysorted", [ROWS, D], BF16, kind=sk)
    dbg_x2 = (nc.dram_tensor("dbg_x2", [T, D], F32, kind="ExternalOutput").ap()
              if debug else None)
    dbgy = (nc.dram_tensor("dbgy", [2, 128, NTT, D], F32, kind="ExternalOutput").ap()
            if debug else None)
    dbgx = (nc.dram_tensor("dbgx", [8, 128, SUB], F32, kind="ExternalOutput").ap()
            if debug else None)
    cin = nc.dram_tensor("cin", [2, D], F32)
    cout = nc.dram_tensor("cout", [8, D], F32)

    EMIT = {k: [] for k in ("sync", "vector", "scalar", "tensor", "gpsimd")}
    MARKS = {}

    def em(eng, fn):
        EMIT[eng].append(fn)

    def wait(eng, ctr, val=None):
        v = ctr.v if val is None else val
        em(eng, lambda e, s=ctr.h, v=v: e.wait_ge(s, v))

    def op(eng, f, ctr=None, n=1):
        if ctr is None:
            em(eng, f)
            return None
        v = ctr.inc(n)
        em(eng, lambda e, f=f, s=ctr.h, n=n: f(e).then_inc(s, n))
        return v

    with ExitStack() as CTX:
        sbuf = lambda n, s, d: CTX.enter_context(nc.sbuf_tensor(n, s, d))
        mksem = lambda n: Ctr(CTX.enter_context(nc.semaphore(n)))

        sLD = mksem("sLD")
        sTP = mksem("sTP")
        sWT = mksem("sWT")
        sCC = mksem("sCC")
        sV = mksem("sV")
        sS = mksem("sS")
        sT = mksem("sT")
        sO = mksem("sO")

        def snap():
            return ({k: len(v) for k, v in EMIT.items()},
                    {"sO": sO.v, "sLD": sLD.v})

        def barrier_t():
            wait("tensor", sV, sV.v)
            wait("tensor", sS, sS.v)

        def vnop(k=2):
            for _ in range(k):
                em("vector", lambda e: e.engine_nop())

        def snop(k=1):
            for _ in range(k):
                em("scalar", lambda e: e.copy(out=sjunk[:, 0:64],
                                              in_=sjunk[:, 64:128]))

        def emit_ln4(src_fn, dst_fn, bst, st2):
            for tt in range(NTT):
                op("vector", lambda e, tt=tt: e.bn_stats(
                    out=bst[:, tt, 0:6], in_=src_fn(tt)[:, 0:512]), sV)
                op("vector", lambda e, tt=tt: e.bn_stats(
                    out=bst[:, tt, 6:12], in_=src_fn(tt)[:, 512:1024]), sV)
            for tt in range(NTT):
                op("vector", lambda e, tt=tt: e.bn_aggr(
                    out=st2[:, tt, 0:2], in_=bst[:, tt, :]), sV)
            for tt in range(NTT):
                op("vector", lambda e, tt=tt: e.tensor_scalar(
                    out=st2[:, tt, 2:3], in0=st2[:, tt, 1:2],
                    scalar1=float(EPS), scalar2=None, op0=ALU.add), sV)
            vnop(2)
            w_pre = op("vector", lambda e: e.engine_nop(), sV)
            wait("scalar", sV, w_pre)
            op("scalar", lambda e: e.activation(
                out=st2[:, :, 3], in_=st2[:, :, 2], func=AF.Sqrt), sS)
            snop(2)
            w_sq = op("scalar", lambda e: e.copy(
                out=sjunk[:, 0:64], in_=sjunk[:, 64:128]), sS)
            wait("vector", sS, w_sq)
            op("vector", lambda e: e.reciprocal(st2[:, :, 5], st2[:, :, 3]), sV)
            vnop(2)
            for tt in range(NTT):
                op("vector", lambda e, tt=tt: e.tensor_scalar(
                    out=dst_fn(tt), in0=src_fn(tt),
                    scalar1=st2[:, tt, 0:1], scalar2=st2[:, tt, 5:6],
                    op0=ALU.subtract, op1=ALU.mult), sV)

        ident = sbuf("ident", [128, 128], F32)
        ident16 = sbuf("ident16", [128, 128], BF16)
        lfullt = sbuf("lfullt", [128, 128], F32)
        ones128 = sbuf("ones128", [128, 128], F32)
        repl = sbuf("repl", [128, 8, 128], F32)
        iota32 = sbuf("iota32", [128, E], F32)
        maskb = sbuf("maskb", [128, 4], F32)
        lamb = sbuf("lamb", [128, DC], F32)
        lsa = sbuf("lsa", [128, DC], F32)
        lsb_ = sbuf("lsb_", [128, DC], F32)
        ls8 = sbuf("ls8", [128, DC], F32)
        wg16 = sbuf("wg16", [128, DC, E], BF16)
        x2 = sbuf("x2", [128, T // 128, D], F32)
        rbase = sbuf("rbase", [128, 2, E], F32)
        hcarry = sbuf("hcarry", [128, DC], F32)
        pcarry = sbuf("pcarry", [128, DC], F32)
        cfold = sbuf("cfold", [128, 2, DC], F32)
        coutb = sbuf("coutb", [128, 8, DC], F32)
        wrapb = sbuf("wrapb", [128, 4, NSUB, SUB // 16], I16)
        smal = sbuf("smal", [128, 32], F32)
        sjunk = sbuf("sjunk", [128, 128], BF16)
        bstb = sbuf("bstb", [128, NTT, 12], F32)
        st2b = sbuf("st2b", [128, NTT, 8], F32)

        for dst, src in ((ident, cident), (lfullt, clfullt), (ones128, cones),
                         (iota32, ciota), (maskb, cmask), (lamb, lam)):
            op("sync", lambda e, d=dst, s=src: e.dma_start(out=d[:], in_=s), sLD, 16)
        op("sync", lambda e: e.dma_start(
            out=repl[:], in_=crepl.rearrange("g q m -> q g m")), sLD, 16)
        w_consts = sLD.v

        em("gpsimd", lambda e: e.load_library(library_config.mlp))
        op("gpsimd", lambda e: e.dma_start(
            out=wg16[:], in_=wg.rearrange("(dc p) e -> p dc e", p=128)), sWT, 16)
        w_wg = sWT.v

        wait("scalar", sLD, w_consts)
        op("scalar", lambda e: e.activation(out=lsa[:], in_=lamb[:], func=AF.Sigmoid), sS)
        op("scalar", lambda e: e.activation(out=lsb_[:], in_=lsa[:], func=AF.Ln), sS)
        w_ls8_pre = sS.v
        wait("vector", sS, w_ls8_pre)
        op("vector", lambda e: e.tensor_scalar_mul(ls8[:], lsb_[:], float(C_EXP)), sV)
        wait("vector", sLD, w_consts)
        op("vector", lambda e: e.tensor_copy(out=ident16[:], in_=ident[:]), sV)
        op("vector", lambda e: e.tensor_scalar_mul(rbase[:, 0, :], iota32[:], float(CC)), sV)
        op("vector", lambda e: e.memset(hcarry[:], 0.0), sV)
        op("vector", lambda e: e.memset(pcarry[:], 1.0), sV)
        op("vector", lambda e: e.memset(cfold[:], 0.0), sV)
        op("vector", lambda e: e.memset(sjunk[:], 0.0), sV)
        w_init = sV.v

        with nc.sbuf_tensor("zeros", [128, 8, D], BF16) as zeros:
            op("vector", lambda e: e.memset(zeros[:], 0.0), sV)
            w_zeros = sV.v
            wait("sync", sV, w_zeros)
            for blk_i in range(XS_E // 1024):
                op("sync", lambda e, i=blk_i: e.dma_start(
                    out=xsorted.ap()[i * 1024:(i + 1) * 1024, :].rearrange(
                        "(c p) d -> p c d", p=128),
                    in_=zeros[:]), sLD, 16)
            w_prezero = sLD.v

        with ExitStack() as actx:
            asb = lambda n, s, d: actx.enter_context(nc.sbuf_tensor(n, s, d))
            xs = asb("xs", [128, NTT, D], F32)
            ln1t = asb("ln1t", [128, NTT, D], F32)
            ln1f16 = asb("ln1f16", [128, DC, SUB], BF16)
            ln1lo = asb("ln1lo", [128, DC, SUB], BF16)
            hi_tok = asb("hi_tok", [128, NTT, D], BF16)
            lo_tok = asb("lo_tok", [128, NTT, D], BF16)
            ones512 = asb("ones512", [128, SUB], F32)
            X = [None] + [[asb(f"X{j}_{i}", [128, SUB], F32) for i in range(2)]
                          for j in range(1, 8)]
            X = {j: X[j] for j in range(1, 8)}
            Pb = [asb(f"Pb_{i}", [128, SUB], BF16) for i in range(2)]
            wrt16 = asb("wrt16", [128, DC, D], BF16)
            wit16 = asb("wit16", [128, DC, D], BF16)
            op("gpsimd", lambda e: e.dma_start(
                out=wrt16[:], in_=wrt.rearrange("(dc p) e -> p dc e", p=128)), sWT, 16)
            op("gpsimd", lambda e: e.dma_start(
                out=wit16[:], in_=wit.rearrange("(dc p) e -> p dc e", p=128)), sWT, 16)
            w_wrwi = sWT.v
            with (nc.psum_tensor("psT", [128, 2, 128], BF16) as psT,
                  nc.psum_tensor("psR", [128, SUB], F32) as psR,
                  nc.psum_tensor("psI", [128, SUB], F32) as psI):

                op("vector", lambda e: e.memset(ones512[:], 1.0), sV)

                last_transpose_T = 0
                store_par = {0: 0, 1: 0}
                rs_par = {0: 0, 1: 0}
                for s in range(NSUB):
                    if s > 0:
                        wait("sync", sT, last_transpose_T)
                    op("sync", lambda e, s=s: e.dma_start(
                        out=xs[:], in_=xin[s * SUB:(s + 1) * SUB, :].rearrange(
                            "(c p) d -> p c d", p=128)), sLD, 16)
                    w_x = sLD.v
                    wait("vector", sLD, w_x)
                    wait("scalar", sLD, w_x)
                    emit_ln4(lambda tt: xs[:, tt, :],
                             lambda tt: ln1t[:, tt, :],
                             bstb, st2b)
                    w_ln1 = sV.v
                    if s == 0:
                        MARKS["A_ln1"] = snap()

                    wait("scalar", sV, w_ln1)
                    for tt in range(NTT):
                        op("scalar", lambda e, tt=tt: e.copy(
                            out=hi_tok[:, tt, :], in_=ln1t[:, tt, :]), sS)
                    w_hi = sS.v
                    wait("vector", sS, w_hi)
                    for tt in range(NTT):
                        op("vector", lambda e, tt=tt: e.tensor_tensor(
                            out=lo_tok[:, tt, :], in0=ln1t[:, tt, :],
                            in1=hi_tok[:, tt, :], op=ALU.subtract), sV)
                    w_lo = sV.v
                    if debug and s == 0:
                        wait("sync", sV, w_lo)
                        op("sync", lambda e: e.dma_start(out=dbgy[0], in_=xs[:]), sLD, 16)
                        op("sync", lambda e: e.dma_start(out=dbgy[1], in_=ln1t[:]), sLD, 16)
                    if s == 0:
                        MARKS["A_hl"] = snap()

                    barrier_t()
                    ev_v = sV.v
                    for i in range(NTT * DC):
                        tt, dc = divmod(i, DC)
                        if i >= 1:
                            wait("tensor", sV, ev_v + 2 * i)
                        op("tensor", lambda e, tt=tt, dc=dc: e.transpose(
                            psT[:, 0, :], hi_tok[:, tt, dc * 128:(dc + 1) * 128],
                            ident16[:]), sT)
                        w_trh = sT.v
                        op("tensor", lambda e, tt=tt, dc=dc: e.transpose(
                            psT[:, 1, :], lo_tok[:, tt, dc * 128:(dc + 1) * 128],
                            ident16[:]), sT)
                        w_trl = sT.v
                        wait("vector", sT, w_trh)
                        op("vector", lambda e, tt=tt, dc=dc: e.tensor_copy(
                            out=ln1f16[:, dc, tt * 128:(tt + 1) * 128],
                            in_=psT[:, 0, :]), sV)
                        wait("vector", sT, w_trl)
                        op("vector", lambda e, tt=tt, dc=dc: e.tensor_copy(
                            out=ln1lo[:, dc, tt * 128:(tt + 1) * 128],
                            in_=psT[:, 1, :]), sV)
                        if s == 0 and i in (3, 7, 15):
                            MARKS[f"A_t{i}"] = snap()
                    last_transpose_T = sT.v
                    w_fm_v = sV.v
                    if s == 0:
                        MARKS["A_tr"] = snap()

                    wait("tensor", sV, w_fm_v)
                    if s == 0:
                        wait("tensor", sWT, w_wrwi)
                    for ec in range(DC):
                        par = ec % 2
                        if store_par[par]:
                            wait("vector", sLD, store_par[par])
                            wait("scalar", sLD, store_par[par])
                        if rs_par[(ec + 1) % 2]:
                            wait("tensor", sS, rs_par[(ec + 1) % 2])
                        for dc in range(DC):
                            op("tensor", lambda e, ec=ec, dc=dc: e.matmul(
                                psR[:], lhsT=wrt16[:, dc, ec * 128:(ec + 1) * 128],
                                rhs=ln1f16[:, dc, :], start=(dc == 0),
                                stop=(dc == DC - 1)), sT if dc == DC - 1 else None)
                        w_r = sT.v
                        for dc in range(DC):
                            op("tensor", lambda e, ec=ec, dc=dc: e.matmul(
                                psI[:], lhsT=wit16[:, dc, ec * 128:(ec + 1) * 128],
                                rhs=ln1f16[:, dc, :], start=(dc == 0),
                                stop=(dc == DC - 1)), sT if dc == DC - 1 else None)
                        w_i = sT.v
                        wait("scalar", sT, w_r)
                        op("scalar", lambda e, par=par: e.activation(
                            out=X[1][par][:], in_=psR[:], func=AF.Sigmoid), sS)
                        wait("scalar", sT, w_i)
                        op("scalar", lambda e, par=par: e.activation(
                            out=X[2][par][:], in_=psI[:], func=AF.Sigmoid), sS)
                        w_is = sS.v
                        rs_par[par] = w_is
                        wait("vector", sS, w_is)
                        op("vector", lambda e, ec=ec, par=par: e.tensor_scalar_mul(
                            X[3][par][:], X[1][par][:], ls8[:, ec:ec + 1]), sV)
                        w_loga = sV.v
                        wait("scalar", sV, w_loga)
                        op("scalar", lambda e, par=par: e.activation(
                            out=X[4][par][:], in_=X[3][par][:], func=AF.Exp), sS)
                        op("scalar", lambda e, par=par: e.activation(
                            out=X[5][par][:], in_=X[3][par][:], func=AF.Exp,
                            scale=2.0), sS)
                        w_exp = sS.v
                        wait("vector", sS, w_exp)
                        op("vector", lambda e, par=par: e.tensor_scalar(
                            out=X[6][par][:], in0=X[5][par][:], scalar1=-1.0,
                            scalar2=1.0, op0=ALU.mult, op1=ALU.add), sV)
                        op("vector", lambda e, par=par: e.tensor_scalar_max(
                            X[7][par][:], X[6][par][:], 0.0), sV)
                        w_om = sV.v
                        wait("scalar", sV, w_om)
                        op("scalar", lambda e, par=par: e.activation(
                            out=X[5][par][:], in_=X[7][par][:], func=AF.Sqrt), sS)
                        w_t2 = sS.v
                        wait("vector", sS, w_t2)
                        op("vector", lambda e, par=par: e.tensor_tensor(
                            out=X[6][par][:], in0=X[5][par][:], in1=X[2][par][:],
                            op=ALU.mult), sV)
                        op("vector", lambda e, ec=ec, par=par: e.tensor_tensor(
                            out=X[7][par][:], in0=X[6][par][:],
                            in1=ln1f16[:, ec, :], op=ALU.mult), sV)
                        op("vector", lambda e, ec=ec, par=par: e.tensor_tensor(
                            out=X[2][par][:], in0=X[6][par][:],
                            in1=ln1lo[:, ec, :], op=ALU.mult), sV)
                        op("vector", lambda e, par=par: e.tensor_tensor(
                            out=X[1][par][:], in0=X[7][par][:], in1=X[2][par][:],
                            op=ALU.add), sV)
                        op("vector", lambda e, ec=ec, par=par: e.tensor_tensor_scan(
                            out=X[2][par][:], data0=X[4][par][:], data1=ones512[:],
                            initial=pcarry[:, ec:ec + 1], op0=ALU.mult,
                            op1=ALU.mult), sV)
                        op("vector", lambda e, ec=ec, par=par: e.tensor_tensor_scan(
                            out=X[3][par][:], data0=X[4][par][:], data1=X[1][par][:],
                            initial=hcarry[:, ec:ec + 1], op0=ALU.mult,
                            op1=ALU.add), sV)
                        op("vector", lambda e, ec=ec, par=par: e.tensor_copy(
                            out=pcarry[:, ec:ec + 1],
                            in_=X[2][par][:, SUB - 1:SUB]), sV)
                        vnop(2)
                        op("vector", lambda e, ec=ec, par=par: e.tensor_copy(
                            out=hcarry[:, ec:ec + 1],
                            in_=X[3][par][:, SUB - 1:SUB]), sV)
                        w_scan = sV.v
                        if debug and s == 0 and ec == 0:
                            wait("sync", sV, w_scan)
                            for di, buf in enumerate((X[1][par], X[2][par],
                                                      X[3][par], X[4][par],
                                                      X[5][par], X[6][par],
                                                      X[7][par], ones512)):
                                op("sync", lambda e, di=di, buf=buf: e.dma_start(
                                    out=dbgx[di], in_=buf[:]), sLD, 16)
                        wait("scalar", sV, w_scan)
                        op("scalar", lambda e, par=par: e.copy(
                            out=Pb[par][:], in_=X[2][par][:]), sS)
                        w_pb = sS.v
                        wait("sync", sV, w_scan)
                        op("sync", lambda e, s=s, ec=ec, par=par: e.dma_start(
                            out=hbm_h.ap()[ec * 128:(ec + 1) * 128,
                                           s * SUB:(s + 1) * SUB],
                            in_=X[3][par][:]), sLD, 16)
                        wait("sync", sS, w_pb)
                        op("sync", lambda e, s=s, ec=ec, par=par: e.dma_start(
                            out=hbm_p.ap()[ec * 128:(ec + 1) * 128,
                                           s * SUB:(s + 1) * SUB],
                            in_=Pb[par][:]), sLD, 16)
                        store_par[par] = sLD.v
                        if s == 0 and ec == 0:
                            MARKS["A_ec0"] = snap()
                    if s == 0:
                        MARKS["A_s0"] = snap()
                w_a_done_v = sV.v
        MARKS["A"] = snap()

        wait("sync", sV, w_a_done_v)
        op("sync", lambda e: e.dma_start(
            out=cin.ap()[0:1, :].rearrange("r (p dc) -> p (dc r)", p=128),
            in_=pcarry[:]), sLD, 16)
        op("sync", lambda e: e.dma_start(
            out=cin.ap()[1:2, :].rearrange("r (p dc) -> p (dc r)", p=128),
            in_=hcarry[:]), sLD, 16)
        w_cin = sLD.v
        wait("gpsimd", sLD, w_cin)
        op("gpsimd", lambda e: e.collective_compute(
            "AllGather", ALU.bypass,
            replica_groups=[[0, 1, 2, 3], [4, 5, 6, 7]],
            ins=[cin.ap().opt()], outs=[cout.ap().opt()]), sCC)
        wait("sync", sCC, sCC.v)
        op("sync", lambda e: e.dma_start(
            out=coutb[:], in_=cout.ap().rearrange("r (p dc) -> p r dc", p=128)),
            sLD, 16)
        w_cout = sLD.v
        wait("vector", sLD, w_cout)
        for j in range(4):
            ja, jb = j % 2, (j + 1) % 2
            op("vector", lambda e, j=j: e.tensor_scalar(
                out=smal[:, 0:DC], in0=coutb[:, 2 * j, :], scalar1=-1.0,
                scalar2=None, op0=ALU.add), sV)
            op("vector", lambda e, j=j: e.tensor_scalar_mul(
                smal[:, 16:16 + DC], coutb[:, 2 * j + 1, :], maskb[:, j:j + 1]), sV)
            vnop(2)
            op("vector", lambda e, j=j: e.tensor_scalar(
                out=smal[:, 8:8 + DC], in0=smal[:, 0:DC], scalar1=maskb[:, j:j + 1],
                scalar2=1.0, op0=ALU.mult, op1=ALU.add), sV)
            vnop(2)
            op("vector", lambda e, ja=ja: e.tensor_tensor(
                out=smal[:, 24:24 + DC], in0=cfold[:, ja, :], in1=smal[:, 8:8 + DC],
                op=ALU.mult), sV)
            vnop(2)
            op("vector", lambda e, jb=jb: e.tensor_tensor(
                out=cfold[:, jb, :], in0=smal[:, 24:24 + DC],
                in1=smal[:, 16:16 + DC], op=ALU.add), sV)
            vnop(2)
        w_carry = sV.v
        MARKS["C"] = snap()

        scat_done = []
        shared_store = []
        rb_par = [0]
        with ExitStack() as ectx:
            esb = lambda n, s, d: ectx.enter_context(nc.sbuf_tensor(n, s, d))
            he = esb("he", [128, DC, SUB], F32)
            pe = esb("pe", [128, DC, SUB], BF16)
            hfix = esb("hfix", [128, DC, SUB], F32)
            hhi = esb("hhi", [128, DC, SUB], BF16)
            hlo = esb("hlo", [128, DC, SUB], BF16)
            xe = esb("xe", [128, NTT, D], F32)
            htok = esb("htok", [128, NTT, D], F32)
            tmp128 = esb("tmp128", [128, 128], F32)
            ln2b = [esb(f"ln2b_{i}", [128, NTT, D], BF16) for i in range(2)]
            ln2fm = esb("ln2fm", [128, DC, SUB], BF16)
            probs = esb("probs", [128, E], F32)
            gsb = esb("gsb", [128, E], F32)
            mx8 = esb("mx8", [128, 8], F32)
            mi8 = esb("mi8", [128, 8], U32)
            mif = esb("mif", [128, 8], F32)
            ohk = esb("ohk", [128, 4, E], F32)
            Bm = esb("Bm", [128, E], F32)
            Sv = esb("Sv", [128, E], F32)
            Sv2 = esb("Sv2", [128, E], F32)
            slotsub = esb("slotsub", [128, 4, NTT], F32)
            estat = esb("estat", [128, 16], F32)
            X1e = esb("X1e", [128, D], F32)
            with (nc.psum_tensor("psT2", [128, 2, 128], BF16) as psT2,
                  nc.psum_tensor("psG", [128, E], F32) as psG,
                  nc.psum_tensor("psM", [128, E], F32) as psM,
                  nc.psum_tensor("psCS", [128, E], F32) as psCS,
                  nc.psum_tensor("psW", [128, 16], F32) as psW):

                def ln_stats2(buf, idx):
                    op("vector", lambda e: e.reduce_sum(
                        out=estat[:, 0:1], in_=buf[:, idx, :], axis=AX.X), sV)
                    op("vector", lambda e: e.scalar_tensor_tensor(
                        out=X1e[:], in0=buf[:, idx, :], scalar=1.0,
                        in1=buf[:, idx, :], op0=ALU.mult, op1=ALU.mult,
                        accum_out=estat[:, 1:2]), sV)
                    op("vector", lambda e: e.tensor_scalar_mul(
                        estat[:, 2:3], estat[:, 0:1], 1.0 / D), sV)
                    op("vector", lambda e: e.tensor_tensor(
                        out=estat[:, 3:4], in0=estat[:, 2:3], in1=estat[:, 2:3],
                        op=ALU.mult), sV)
                    op("vector", lambda e: e.tensor_scalar(
                        out=estat[:, 4:5], in0=estat[:, 1:2], scalar1=1.0 / D,
                        scalar2=None, op0=ALU.mult), sV)
                    op("vector", lambda e: e.tensor_tensor(
                        out=estat[:, 7:8], in0=estat[:, 4:5], in1=estat[:, 3:4],
                        op=ALU.subtract), sV)
                    op("vector", lambda e: e.tensor_scalar(
                        out=estat[:, 8:9], in0=estat[:, 7:8], scalar1=float(EPS),
                        scalar2=None, op0=ALU.add), sV)
                    wv_ = sV.v
                    wait("scalar", sV, wv_)
                    op("scalar", lambda e: e.activation(
                        out=estat[:, 9:10], in_=estat[:, 8:9], func=AF.Sqrt), sS)
                    ws_ = sS.v
                    wait("vector", sS, ws_)
                    op("vector", lambda e: e.reciprocal(
                        estat[:, 10:11], estat[:, 9:10]), sV)

                w_route_prev = 0
                last_T_prev = 0
                for s in range(NSUB):
                    par = s % 2
                    if s > 0:
                        wait("sync", sV, w_route_prev)
                        wait("sync", sT, last_T_prev)
                    op("sync", lambda e, s=s: e.dma_start(
                        out=he[:], in_=hbm_h.ap()[:, s * SUB:(s + 1) * SUB].rearrange(
                            "(dc p) t -> p dc t", p=128)), sLD, 16)
                    op("sync", lambda e, s=s: e.dma_start(
                        out=pe[:], in_=hbm_p.ap()[:, s * SUB:(s + 1) * SUB].rearrange(
                            "(dc p) t -> p dc t", p=128)), sLD, 16)
                    op("sync", lambda e, s=s: e.dma_start(
                        out=xe[:], in_=xin[s * SUB:(s + 1) * SUB, :].rearrange(
                            "(c p) d -> p c d", p=128)), sLD, 16)
                    w_eload = sLD.v
                    wait("vector", sLD, w_eload)
                    if s == 0:
                        wait("vector", sV, w_carry)
                    for ec in range(DC):
                        op("vector", lambda e, ec=ec: e.scalar_tensor_tensor(
                            out=hfix[:, ec, :], in0=pe[:, ec, :],
                            scalar=cfold[:, 0, ec:ec + 1], in1=he[:, ec, :],
                            op0=ALU.mult, op1=ALU.add), sV)
                    w_fix = sV.v
                    wait("scalar", sV, w_fix)
                    for ec in range(DC):
                        op("scalar", lambda e, ec=ec: e.copy(
                            out=hhi[:, ec, :], in_=hfix[:, ec, :]), sS)
                    w_hhi = sS.v
                    wait("vector", sS, w_hhi)
                    for ec in range(DC):
                        op("vector", lambda e, ec=ec: e.tensor_tensor(
                            out=hlo[:, ec, :], in0=hfix[:, ec, :],
                            in1=hhi[:, ec, :], op=ALU.subtract), sV)
                    w_hlo = sV.v
                    barrier_t()
                    ev_v = sV.v
                    for i in range(NTT * DC):
                        tt, dc = divmod(i, DC)
                        if i >= 1:
                            wait("tensor", sV, ev_v + 2 * i)
                        op("tensor", lambda e, tt=tt, dc=dc: e.transpose(
                            psT2[:, 0, :], hhi[:, dc, tt * 128:(tt + 1) * 128],
                            ident16[:]), sT)
                        w_trh = sT.v
                        op("tensor", lambda e, tt=tt, dc=dc: e.transpose(
                            psT2[:, 1, :], hlo[:, dc, tt * 128:(tt + 1) * 128],
                            ident16[:]), sT)
                        w_trl = sT.v
                        wait("vector", sT, w_trh)
                        op("vector", lambda e: e.tensor_copy(
                            out=tmp128[:], in_=psT2[:, 0, :]), sV)
                        wait("vector", sT, w_trl)
                        op("vector", lambda e, tt=tt, dc=dc: e.tensor_tensor(
                            out=htok[:, tt, dc * 128:(dc + 1) * 128],
                            in0=tmp128[:], in1=psT2[:, 1, :], op=ALU.add), sV)
                    w_htok = sV.v

                    hfv = hfix[:].rearrange("p a b -> p (a b)")
                    hev = he[:].rearrange("p a b -> p (a b)")
                    emit_ln4(lambda tt: htok[:, tt, :],
                             lambda tt: hfv[:, tt * D:(tt + 1) * D],
                             bstb, st2b)
                    for tt in range(NTT):
                        op("vector", lambda e, s=s, tt=tt, hfv=hfv: e.tensor_tensor(
                            out=x2[:, s * NTT + tt, :], in0=xe[:, tt, :],
                            in1=hfv[:, tt * D:(tt + 1) * D], op=ALU.add), sV)
                    if len(scat_done) >= 1:
                        wait("vector", sWT, scat_done[-1])
                    if len(shared_store) >= 2:
                        wait("vector", sLD, shared_store[-2])
                    emit_ln4(lambda tt, s=s: x2[:, s * NTT + tt, :],
                             lambda tt, par=par: ln2b[par][:, tt, :],
                             bstb, st2b)
                    w_ln2b = sV.v

                    barrier_t()
                    ev2 = sS.v
                    for i in range(NTT * DC):
                        tt, dc = divmod(i, DC)
                        tpar = i % 2
                        if i >= 2:
                            wait("tensor", sS, ev2 + (i - 2) + 1)
                        op("tensor", lambda e, tt=tt, dc=dc, tpar=tpar, par=par:
                           e.transpose(
                            psT2[:, tpar, :],
                            ln2b[par][:, tt, dc * 128:(dc + 1) * 128],
                            ident16[:]), sT)
                        w_tr = sT.v
                        wait("scalar", sT, w_tr)
                        op("scalar", lambda e, tt=tt, dc=dc, tpar=tpar: e.copy(
                            out=ln2fm[:, dc, tt * 128:(tt + 1) * 128],
                            in_=psT2[:, tpar, :]), sS)
                    w_ln2fm = sS.v

                    wait("tensor", sS, w_ln2fm)
                    if s == 0:
                        wait("tensor", sV, w_init)
                        wait("tensor", sWT, w_wg)
                    w_pr_prev = 0
                    w_consume_prev = 0
                    for tt in range(NTT):
                        if w_pr_prev:
                            wait("tensor", sV, w_consume_prev)
                            wait("tensor", sS, w_pr_prev)
                        for dc in range(DC):
                            op("tensor", lambda e, tt=tt, dc=dc: e.matmul(
                                psG[:], lhsT=ln2fm[:, dc, tt * 128:(tt + 1) * 128],
                                rhs=wg16[:, dc, :], start=(dc == 0),
                                stop=(dc == DC - 1)), sT if dc == DC - 1 else None)
                        w_g = sT.v
                        wait("vector", sT, w_g)
                        op("vector", lambda e: e.tensor_copy(
                            out=gsb[:], in_=psG[:]), sV)
                        vnop(2)
                        w_gsb = op("vector", lambda e: e.engine_nop(), sV)
                        wait("scalar", sV, w_gsb)
                        op("scalar", lambda e: e.activation(
                            out=probs[:], in_=gsb[:], func=AF.Exp), sS)
                        snop(2)
                        w_pr = op("scalar", lambda e: e.copy(
                            out=sjunk[:, 0:64], in_=sjunk[:, 64:128]), sS)
                        w_pr_prev = w_pr
                        wait("vector", sS, w_pr)
                        op("vector", lambda e: e.max(out=mx8[:], in_=probs[:]), sV)
                        vnop(2)
                        op("vector", lambda e: e.max_index(
                            out=mi8[:], in_max=mx8[:], in_values=probs[:]), sV)
                        vnop(2)
                        op("vector", lambda e: e.tensor_copy(out=mif[:], in_=mi8[:]), sV)
                        vnop(2)
                        for k in range(4):
                            op("vector", lambda e, k=k: e.tensor_tensor(
                                out=ohk[:, k, :], in0=iota32[:],
                                in1=mif[:, k:k + 1].to_broadcast([128, E]),
                                op=ALU.is_equal), sV)
                        op("vector", lambda e: e.tensor_tensor(
                            out=Bm[:], in0=ohk[:, 0, :], in1=ohk[:, 1, :],
                            op=ALU.add), sV)
                        op("vector", lambda e: e.tensor_tensor(
                            out=Sv[:], in0=ohk[:, 2, :], in1=ohk[:, 3, :],
                            op=ALU.add), sV)
                        vnop(2)
                        op("vector", lambda e: e.tensor_tensor(
                            out=Bm[:], in0=Bm[:], in1=Sv[:], op=ALU.add), sV)
                        vnop(2)
                        w_B = op("vector", lambda e: e.engine_nop(), sV)
                        wait("tensor", sV, w_B)
                        op("tensor", lambda e: e.matmul(
                            psM[:], lhsT=lfullt[:], rhs=Bm[:], start=True,
                            stop=True), sT)
                        op("tensor", lambda e: e.matmul(
                            psCS[:], lhsT=ones128[:], rhs=Bm[:], start=True,
                            stop=True), sT)
                        w_M = sT.v
                        rp = rb_par[0]
                        wait("vector", sT, w_M)
                        op("vector", lambda e: e.tensor_tensor(
                            out=Sv[:], in0=psM[:], in1=Bm[:], op=ALU.subtract), sV)
                        vnop(2)
                        op("vector", lambda e, rp=rp: e.tensor_tensor(
                            out=Sv2[:], in0=Sv[:], in1=rbase[:, rp, :],
                            op=ALU.add), sV)
                        vnop(2)
                        for k in range(4):
                            op("vector", lambda e, k=k, tt=tt: e.scalar_tensor_tensor(
                                out=Bm[:], in0=Sv2[:], scalar=1.0, in1=ohk[:, k, :],
                                op0=ALU.mult, op1=ALU.mult,
                                accum_out=slotsub[:, k, tt:tt + 1]), sV)
                        op("vector", lambda e, rp=rp: e.tensor_tensor(
                            out=rbase[:, 1 - rp, :], in0=rbase[:, rp, :],
                            in1=psCS[:], op=ALU.add), sV)
                        rb_par[0] = 1 - rp
                        w_consume_prev = sV.v
                    w_route = sV.v
                    w_route_prev = w_route

                    wait("tensor", sV, w_route)
                    for g in range(8):
                        op("tensor", lambda e, g=g: e.matmul(
                            psW[:], lhsT=repl[:, g, :],
                            rhs=slotsub[:].rearrange("p k tt -> p (k tt)"),
                            start=True, stop=True), sT)
                        w_w = sT.v
                        wait("vector", sT, w_w)
                        for k in range(4):
                            op("vector", lambda e, g=g, k=k, s=s: e.tensor_copy(
                                out=wrapb[:, k, s, g::8],
                                in_=psW[:, k * NTT:(k + 1) * NTT]), sV)
                        w_wc = sV.v
                        if g < 7:
                            wait("tensor", sV, w_wc)
                    w_wrap = sV.v

                    wait("gpsimd", sV, w_wrap)
                    wait("gpsimd", sV, w_ln2b)
                    if s == 0:
                        wait("gpsimd", sLD, w_prezero)
                    for k in range(4):
                        op("gpsimd", lambda e, k=k, s=s, par=par: e.dma_scatter_add(
                            out_ap=xsorted.ap(), in_ap=ln2b[par][:],
                            idxs_ap=wrapb[:, k, s, :], num_idxs=SUB,
                            num_idxs_reg=SUB, elem_size=D), sWT, 16)
                    scat_done.append(sWT.v)
                    wait("sync", sV, w_ln2b)
                    op("sync", lambda e, s=s, par=par: e.dma_start(
                        out=xsorted.ap()[XS_E + s * SUB: XS_E + (s + 1) * SUB, :]
                        .rearrange("(c p) d -> p c d", p=128),
                        in_=ln2b[par][:]), sLD, 16)
                    shared_store.append(sLD.v)
                    last_T_prev = sT.v
            w_scatters = sWT.v
        MARKS["E"] = snap()

        yst_waits = []
        with ExitStack() as sctx:
            ssb = lambda n, s, d: sctx.enter_context(nc.sbuf_tensor(n, s, d))
            w1b = [ssb(f"w1b{i}", [128, DC, H], BF16) for i in range(2)]
            w3b = [ssb(f"w3b{i}", [128, DC, H], BF16) for i in range(2)]
            w2b = [ssb(f"w2b{i}", [128, HC, D], BF16) for i in range(2)]
            xg = [ssb(f"xg{i}", [128, DC, CC], BF16) for i in range(2)]
            wbuf = [ssb(f"wbuf{i}", [128, H], BF16) for i in range(2)]
            wbufT = [ssb(f"wbufT{i}", [128, HC, 128], BF16) for i in range(2)]
            yst = [ssb(f"yst{i}", [128, D], BF16) for i in range(2)]
            silu_u = ssb("silu_u", [128, H], BF16)
            gstat = ssb("gstat", [128, 8], F32)
            gsb2 = ssb("gsb2", [128, E], F32)
            smal2 = ssb("smal2", [128, E], F32)
            with (nc.psum_tensor("psU", [128, H], F32) as psU,
                  nc.psum_tensor("psV_", [128, H], F32) as psV_,
                  nc.psum_tensor("psY", [128, 2, 512], F32) as psY,
                  nc.psum_tensor("psG2", [128, E], F32) as psG2,
                  nc.psum_tensor("psT3", [128, 2, 128], BF16) as psT3):

                wload_done = {}
                xg_done = {}

                def emit_wload(ei):
                    p_ = ei % 2
                    op("gpsimd", lambda e, i=ei, p_=p_: e.dma_start(
                        out=w1b[p_][:],
                        in_=w1[i].rearrange("(dc p) h -> p dc h", p=128)), sWT, 16)
                    op("gpsimd", lambda e, i=ei, p_=p_: e.dma_start(
                        out=w3b[p_][:],
                        in_=w3[i].rearrange("(dc p) h -> p dc h", p=128)), sWT, 16)
                    op("gpsimd", lambda e, i=ei, p_=p_: e.dma_start(
                        out=w2b[p_][:],
                        in_=w2[i].rearrange("(hc p) d -> p hc d", p=128)), sWT, 16)
                    wload_done[ei] = sWT.v

                def emit_xgload(ei):
                    p_ = ei % 2
                    wait("scalar", sTP, sTP.v)
                    op("scalar", lambda e, i=ei, p_=p_: e.dma_start(
                        out=xg[p_][:], in_=xsorted.ap()[i * CC:(i + 1) * CC, :],
                        transpose=True), sTP, 16)
                    xg_done[ei] = sTP.v

                wait("gpsimd", sWT, w_scatters)
                emit_wload(0)
                wait("scalar", sWT, w_scatters)
                wait("scalar", sLD, shared_store[-1])
                emit_xgload(0)
                emit_wload(1)
                emit_xgload(1)

                w_su_prev = 0
                w_wb_prev = 0
                w_ex_prev = 0
                w_g2c_prev = 0
                w_y1_prev = 0
                for ei in range(E):
                    par = ei % 2
                    wait("tensor", sWT, wload_done[ei])
                    wait("tensor", sTP, xg_done[ei])
                    for mt in range(CC // 128):
                        m0 = mt * 128
                        ypar = (ei * (CC // 128) + mt) % 2
                        if w_su_prev:
                            wait("tensor", sS, w_su_prev)
                        for dc in range(DC):
                            op("tensor", lambda e, dc=dc, par=par, m0=m0: e.matmul(
                                psU[:], lhsT=xg[par][:, dc, m0:m0 + 128],
                                rhs=w1b[par][:, dc, :], start=(dc == 0),
                                stop=(dc == DC - 1)), sT if dc == DC - 1 else None)
                        w_u = sT.v
                        if w_wb_prev:
                            wait("tensor", sV, w_wb_prev)
                        for dc in range(DC):
                            op("tensor", lambda e, dc=dc, par=par, m0=m0: e.matmul(
                                psV_[:], lhsT=xg[par][:, dc, m0:m0 + 128],
                                rhs=w3b[par][:, dc, :], start=(dc == 0),
                                stop=(dc == DC - 1)), sT if dc == DC - 1 else None)
                        w_v = sT.v
                        if w_g2c_prev:
                            wait("tensor", sV, w_g2c_prev)
                            wait("tensor", sS, w_ex_prev)
                        for dc in range(DC):
                            op("tensor", lambda e, dc=dc, par=par, m0=m0: e.matmul(
                                psG2[:], lhsT=xg[par][:, dc, m0:m0 + 128],
                                rhs=wg16[:, dc, :], start=(dc == 0),
                                stop=(dc == DC - 1)), sT if dc == DC - 1 else None)
                        w_g2 = sT.v
                        wait("scalar", sT, w_u)
                        op("scalar", lambda e: e.activation(
                            out=silu_u[:], in_=psU[:], func=AF.Silu), sS)
                        w_su = sS.v
                        w_su_prev = w_su
                        wait("vector", sS, w_su)
                        wait("vector", sT, w_v)
                        op("vector", lambda e, ypar=ypar: e.tensor_tensor(
                            out=wbuf[ypar][:], in0=silu_u[:], in1=psV_[:],
                            op=ALU.mult), sV)
                        w_wb = sV.v
                        w_wb_prev = w_wb
                        wait("vector", sT, w_g2)
                        op("vector", lambda e: e.tensor_copy(
                            out=gsb2[:], in_=psG2[:]), sV)
                        vnop(2)
                        w_g2c = op("vector", lambda e: e.engine_nop(), sV)
                        w_g2c_prev = w_g2c
                        wait("scalar", sV, w_g2c)
                        op("scalar", lambda e: e.activation(
                            out=smal2[:], in_=gsb2[:], func=AF.Exp), sS)
                        snop(2)
                        w_ex = op("scalar", lambda e: e.copy(
                            out=sjunk[:, 0:64], in_=sjunk[:, 64:128]), sS)
                        w_ex_prev = w_ex
                        wait("vector", sS, w_ex)
                        op("vector", lambda e: e.reduce_sum(
                            out=gstat[:, 2:3], in_=smal2[:], axis=AX.X), sV)
                        vnop(2)
                        op("vector", lambda e: e.reciprocal(
                            gstat[:, 3:4], gstat[:, 2:3]), sV)
                        vnop(2)
                        op("vector", lambda e, ei=ei: e.tensor_tensor(
                            out=gstat[:, 4:5], in0=smal2[:, ei:ei + 1],
                            in1=gstat[:, 3:4], op=ALU.mult), sV)
                        vnop(2)
                        w_wgt = op("vector", lambda e: e.engine_nop(), sV)
                        wait("tensor", sV, w_wb)
                        for hc in range(HC):
                            tp = hc % 2
                            if hc >= 2:
                                wait("tensor", sV, sV.v)
                            op("tensor", lambda e, hc=hc, tp=tp, ypar=ypar:
                               e.transpose(
                                psT3[:, tp, :],
                                wbuf[ypar][:, hc * 128:(hc + 1) * 128],
                                ident16[:]), sT)
                            w_tr = sT.v
                            wait("vector", sT, w_tr)
                            op("vector", lambda e, hc=hc, tp=tp, ypar=ypar:
                               e.tensor_copy(out=wbufT[ypar][:, hc, :],
                                             in_=psT3[:, tp, :]), sV)
                        w_wt = sV.v
                        wait("tensor", sV, w_wt)
                        if w_y1_prev:
                            wait("tensor", sV, w_y1_prev)
                        for half in range(2):
                            for hc in range(HC):
                                op("tensor", lambda e, half=half, hc=hc, ypar=ypar,
                                   par=par: e.matmul(
                                    psY[:, half, :], lhsT=wbufT[ypar][:, hc, :],
                                    rhs=w2b[par][:, hc, half * 512:(half + 1) * 512],
                                    start=(hc == 0), stop=(hc == HC - 1)),
                                    sT if hc == HC - 1 else None)
                        w_y = sT.v
                        if mt == CC // 128 - 1 and ei + 2 < E:
                            wait("gpsimd", sT, w_y)
                            emit_wload(ei + 2)
                            wait("scalar", sT, w_y)
                            emit_xgload(ei + 2)
                        wait("vector", sT, w_y)
                        wait("vector", sV, w_wgt)
                        op("vector", lambda e, ypar=ypar: e.tensor_tensor(
                            out=yst[ypar][:],
                            in0=psY[:].rearrange("p a b -> p (a b)"),
                            in1=gstat[:, 4:5].to_broadcast([128, D]),
                            op=ALU.mult), sV)
                        w_y1 = sV.v
                        w_y1_prev = w_y1
                        wait("sync", sV, w_y1)
                        row0 = ei * CC + m0
                        op("sync", lambda e, row0=row0, ypar=ypar: e.dma_start(
                            out=ysorted.ap()[row0:row0 + 128, :],
                            in_=yst[ypar][:]), sLD, 16)
                        yst_waits.append(sLD.v)
                        if len(yst_waits) >= 2:
                            wait("vector", sLD, yst_waits[-2])
        MARKS["S"] = snap()

        with ExitStack() as shctx:
            hsb = lambda n, s, d: shctx.enter_context(nc.sbuf_tensor(n, s, d))
            sw1 = hsb("sw1", [128, DC, HS], BF16)
            sw3 = hsb("sw3", [128, DC, HS], BF16)
            sw2 = hsb("sw2", [128, DC, D], BF16)
            sxg = [hsb(f"sxg{i}", [128, DC, 512], BF16) for i in range(2)]
            swb = [hsb(f"swb{i}", [128, HS], BF16) for i in range(2)]
            swbT = [hsb(f"swbT{i}", [128, DC, 128], BF16) for i in range(2)]
            syst = [hsb(f"syst{i}", [128, D], BF16) for i in range(2)]
            ssilu = hsb("ssilu", [128, HS], BF16)
            with (nc.psum_tensor("psU2", [128, 2, 512], F32) as psU2,
                  nc.psum_tensor("psY2", [128, 2, 512], F32) as psY2,
                  nc.psum_tensor("psT4", [128, 2, 128], BF16) as psT4):

                wait("gpsimd", sT, sT.v)
                wait("gpsimd", sS, sS.v)
                wait("gpsimd", sV, sV.v)
                op("gpsimd", lambda e: e.dma_start(
                    out=sw1[:], in_=ws1.rearrange("(dc p) h -> p dc h", p=128)),
                    sWT, 16)
                op("gpsimd", lambda e: e.dma_start(
                    out=sw3[:], in_=ws3.rearrange("(dc p) h -> p dc h", p=128)),
                    sWT, 16)
                op("gpsimd", lambda e: e.dma_start(
                    out=sw2[:], in_=ws2.rearrange("(hc p) d -> p hc d", p=128)),
                    sWT, 16)
                w_sw = sWT.v
                sxg_done = []
                for q in range(2):
                    wait("scalar", sTP, sTP.v)
                    wait("scalar", sT, sT.v)
                    op("scalar", lambda e, q=q: e.dma_start(
                        out=sxg[q][:],
                        in_=xsorted.ap()[XS_E + q * 512: XS_E + (q + 1) * 512, :],
                        transpose=True), sTP, 16)
                    sxg_done.append(sTP.v)
                wait("tensor", sWT, w_sw)
                w_su_prev = 0
                w_wb_prev = 0
                w_y1_prev = 0
                for q in range(T // 512):
                    qpar = q % 2
                    if q >= 2:
                        wait("scalar", sT, sT.v)
                        wait("scalar", sTP, sTP.v)
                        op("scalar", lambda e, q=q, qpar=qpar: e.dma_start(
                            out=sxg[qpar][:],
                            in_=xsorted.ap()[XS_E + q * 512: XS_E + (q + 1) * 512, :],
                            transpose=True), sTP, 16)
                        sxg_done.append(sTP.v)
                    wait("tensor", sTP, sxg_done[q])
                    for mt in range(4):
                        m0 = mt * 128
                        ypar = mt % 2
                        if w_su_prev:
                            wait("tensor", sS, w_su_prev)
                        for hh in range(2):
                            for dc in range(DC):
                                op("tensor", lambda e, hh=hh, dc=dc, qpar=qpar,
                                   m0=m0: e.matmul(
                                    psU2[:, hh, :],
                                    lhsT=sxg[qpar][:, dc, m0:m0 + 128],
                                    rhs=sw1[:, dc, hh * 512:(hh + 1) * 512],
                                    start=(dc == 0), stop=(dc == DC - 1)),
                                    sT if dc == DC - 1 else None)
                        w_u = sT.v
                        wait("scalar", sT, w_u)
                        op("scalar", lambda e: e.activation(
                            out=ssilu[:], in_=psU2[:].rearrange("p a b -> p (a b)"),
                            func=AF.Silu), sS)
                        w_su = sS.v
                        w_su_prev = w_su
                        if w_wb_prev:
                            wait("tensor", sV, w_wb_prev)
                        if w_y1_prev:
                            wait("tensor", sV, w_y1_prev)
                        for hh in range(2):
                            for dc in range(DC):
                                op("tensor", lambda e, hh=hh, dc=dc, qpar=qpar,
                                   m0=m0: e.matmul(
                                    psY2[:, hh, :],
                                    lhsT=sxg[qpar][:, dc, m0:m0 + 128],
                                    rhs=sw3[:, dc, hh * 512:(hh + 1) * 512],
                                    start=(dc == 0), stop=(dc == DC - 1)),
                                    sT if dc == DC - 1 else None)
                        w_v = sT.v
                        wait("vector", sS, w_su)
                        wait("vector", sT, w_v)
                        op("vector", lambda e, ypar=ypar: e.tensor_tensor(
                            out=swb[ypar][:], in0=ssilu[:],
                            in1=psY2[:].rearrange("p a b -> p (a b)"),
                            op=ALU.mult), sV)
                        w_wb = sV.v
                        w_wb_prev = w_wb
                        wait("tensor", sV, w_wb)
                        for hc in range(DC):
                            tp = hc % 2
                            if hc >= 2:
                                wait("tensor", sV, sV.v)
                            op("tensor", lambda e, hc=hc, tp=tp, ypar=ypar:
                               e.transpose(psT4[:, tp, :],
                                           swb[ypar][:, hc * 128:(hc + 1) * 128],
                                           ident16[:]), sT)
                            w_tr = sT.v
                            wait("vector", sT, w_tr)
                            op("vector", lambda e, hc=hc, tp=tp, ypar=ypar:
                               e.tensor_copy(out=swbT[ypar][:, hc, :],
                                             in_=psT4[:, tp, :]), sV)
                        w_wt = sV.v
                        wait("tensor", sV, w_wt)
                        for half in range(2):
                            for hc in range(DC):
                                op("tensor", lambda e, half=half, hc=hc, ypar=ypar:
                                   e.matmul(
                                    psY2[:, half, :], lhsT=swbT[ypar][:, hc, :],
                                    rhs=sw2[:, hc, half * 512:(half + 1) * 512],
                                    start=(hc == 0), stop=(hc == DC - 1)),
                                    sT if hc == DC - 1 else None)
                        w_y = sT.v
                        wait("vector", sT, w_y)
                        op("vector", lambda e, ypar=ypar: e.tensor_copy(
                            out=syst[ypar][:].rearrange("p (a b) -> p a b", a=2),
                            in_=psY2[:]), sV)
                        w_y1 = sV.v
                        w_y1_prev = w_y1
                        wait("sync", sV, w_y1)
                        row0 = XS_E + q * 512 + m0
                        op("sync", lambda e, row0=row0, ypar=ypar: e.dma_start(
                            out=ysorted.ap()[row0:row0 + 128, :],
                            in_=syst[ypar][:]), sLD, 16)
                        yst_waits.append(sLD.v)
                        if len(yst_waits) >= 2:
                            wait("vector", sLD, yst_waits[-2])
                w_allstores = sLD.v
        MARKS["H"] = snap()

        with ExitStack() as gctx:
            galloc = lambda n, s, d: gctx.enter_context(nc.sbuf_tensor(n, s, d))
            gk = [galloc(f"gk{k}", [128, NTT, D], BF16) for k in range(4)]
            zb = galloc("zb", [128, NTT, D], BF16)
            acc = [galloc(f"acc{i}", [128, NTT, D], F32) for i in range(2)]
            outb = galloc("outb", [128, NTT, D], F32)
            gstat2 = galloc("gstat2", [128, 16], F32)
            Xg2 = galloc("Xg2", [128, D], F32)

            def ln_stats3(buf, idx):
                op("vector", lambda e: e.reduce_sum(
                    out=gstat2[:, 0:1], in_=buf[:, idx, :], axis=AX.X), sV)
                op("vector", lambda e: e.scalar_tensor_tensor(
                    out=Xg2[:], in0=buf[:, idx, :], scalar=1.0,
                    in1=buf[:, idx, :], op0=ALU.mult, op1=ALU.mult,
                    accum_out=gstat2[:, 1:2]), sV)
                op("vector", lambda e: e.tensor_scalar_mul(
                    gstat2[:, 2:3], gstat2[:, 0:1], 1.0 / D), sV)
                op("vector", lambda e: e.tensor_tensor(
                    out=gstat2[:, 3:4], in0=gstat2[:, 2:3], in1=gstat2[:, 2:3],
                    op=ALU.mult), sV)
                op("vector", lambda e: e.tensor_scalar(
                    out=gstat2[:, 4:5], in0=gstat2[:, 1:2], scalar1=1.0 / D,
                    scalar2=None, op0=ALU.mult), sV)
                op("vector", lambda e: e.tensor_tensor(
                    out=gstat2[:, 7:8], in0=gstat2[:, 4:5], in1=gstat2[:, 3:4],
                    op=ALU.subtract), sV)
                op("vector", lambda e: e.tensor_scalar(
                    out=gstat2[:, 8:9], in0=gstat2[:, 7:8], scalar1=float(EPS),
                    scalar2=None, op0=ALU.add), sV)
                wr = sV.v
                wait("scalar", sV, wr)
                op("scalar", lambda e: e.activation(
                    out=gstat2[:, 9:10], in_=gstat2[:, 8:9], func=AF.Sqrt), sS)
                ws_ = sS.v
                wait("vector", sS, ws_)
                op("vector", lambda e: e.reciprocal(
                    gstat2[:, 10:11], gstat2[:, 9:10]), sV)

            wait("gpsimd", sLD, w_allstores)
            w_acc_prev = 0
            for s in range(NSUB):
                if s > 0:
                    wait("gpsimd", sV, w_acc_prev)
                    wait("sync", sV, w_acc_prev)
                for k in range(4):
                    op("gpsimd", lambda e, k=k, s=s: e.dma_gather(
                        out_ap=gk[k][:], in_ap=ysorted.ap(),
                        idxs_ap=wrapb[:, k, s, :], num_idxs=SUB, num_idxs_reg=SUB,
                        elem_size=D), sWT, 16)
                w_gk = sWT.v
                if s == 0:
                    wait("sync", sLD, w_allstores)
                op("sync", lambda e, s=s: e.dma_start(
                    out=zb[:],
                    in_=ysorted.ap()[XS_E + s * SUB: XS_E + (s + 1) * SUB, :]
                    .rearrange("(c p) d -> p c d", p=128)), sLD, 16)
                w_z = sLD.v
                wait("vector", sWT, w_gk)
                wait("vector", sLD, w_z)
                op("vector", lambda e: e.tensor_tensor(
                    out=acc[0][:], in0=gk[0][:], in1=gk[1][:], op=ALU.add), sV)
                op("vector", lambda e: e.tensor_tensor(
                    out=acc[1][:], in0=acc[0][:], in1=gk[2][:], op=ALU.add), sV)
                op("vector", lambda e: e.tensor_tensor(
                    out=acc[0][:], in0=acc[1][:], in1=gk[3][:], op=ALU.add), sV)
                op("vector", lambda e: e.tensor_tensor(
                    out=acc[1][:], in0=acc[0][:], in1=zb[:], op=ALU.add), sV)
                w_acc_prev = sV.v
                emit_ln4(lambda tt: acc[1][:, tt, :],
                         lambda tt: acc[0][:, tt, :],
                         bstb, st2b)
                for tt in range(NTT):
                    op("vector", lambda e, s=s, tt=tt: e.tensor_tensor(
                        out=outb[:, tt, :], in0=x2[:, s * NTT + tt, :],
                        in1=acc[0][:, tt, :], op=ALU.add), sV)
                w_out = sV.v
                wait("sync", sV, w_out)
                op("sync", lambda e, s=s: e.dma_start(
                    out=out[s * SUB:(s + 1) * SUB, :].rearrange(
                        "(c p) d -> p c d", p=128),
                    in_=outb[:]), sO, 16)
                if debug:
                    op("sync", lambda e, s=s: e.dma_start(
                        out=dbg_x2[s * SUB:(s + 1) * SUB, :].rearrange(
                            "(c p) d -> p c d", p=128),
                        in_=x2[:, s * NTT:(s + 1) * NTT, :]), sO, 16)
                if s < NSUB - 1:
                    wait("vector", sO, sO.v)

        MARKS["G"] = snap()
        cut, fin = MARKS[stage] if isinstance(stage, str) else MARKS[
            {1: "A", 2: "C", 3: "E", 4: "S", 5: "H", 6: "G"}[stage]]
        for k in EMIT:
            EMIT[k] = EMIT[k][:cut[k]]
        with nc.Block() as blk:
            @blk.sync
            def _(e):
                for f in EMIT["sync"]:
                    f(e)
                e.wait_ge(sO.h, fin["sO"])
                e.wait_ge(sLD.h, fin["sLD"])

            @blk.vector
            def _(e):
                for f in EMIT["vector"]:
                    f(e)

            @blk.scalar
            def _(e):
                for f in EMIT["scalar"]:
                    f(e)

            @blk.tensor
            def _(e):
                for f in EMIT["tensor"]:
                    f(e)

            @blk.gpsimd
            def _(e):
                for f in EMIT["gpsimd"]:
                    f(e)

    return nc


def make_inputs(full):
    x = np.ascontiguousarray(
        np.asarray(full["L0_attn_x"], np.float32).reshape(2 * 8192, D))
    wrt = np.ascontiguousarray(np.asarray(full["L0_attn_W_r"], np.float32).T)
    wit = np.ascontiguousarray(np.asarray(full["L0_attn_W_i"], np.float32).T)
    lam = np.ascontiguousarray(
        np.asarray(full["L0_attn_lambd"], np.float32).reshape(DC, 128).T)
    wg = np.ascontiguousarray(np.asarray(full["L0_ffn_Wg"], np.float32))
    w1 = np.ascontiguousarray(np.asarray(full["L0_ffn_W1"], np.float32))
    w2 = np.ascontiguousarray(np.asarray(full["L0_ffn_W2"], np.float32))
    w3 = np.ascontiguousarray(np.asarray(full["L0_ffn_W3"], np.float32))
    ws1 = np.ascontiguousarray(np.asarray(full["L0_ffn_Ws1"], np.float32))
    ws2 = np.ascontiguousarray(np.asarray(full["L0_ffn_Ws2"], np.float32))
    ws3 = np.ascontiguousarray(np.asarray(full["L0_ffn_Ws3"], np.float32))
    ident = np.eye(128, dtype=np.float32)
    lfullt = np.triu(np.ones((128, 128), np.float32))
    ones = np.ones((128, 128), np.float32)
    repl = np.zeros((8, 128, 128), np.float32)
    for g in range(8):
        for m in range(128):
            repl[g, 16 * g + m % 16, m] = 1.0
    iota = np.tile(np.arange(E, dtype=np.float32)[None, :], (128, 1))
    maps = []
    for k in range(NCORES):
        rk = k % 4
        cm = np.zeros((128, 4), np.float32)
        cm[:, :rk] = 1.0
        maps.append(dict(
            xin=np.ascontiguousarray(x[k * T:(k + 1) * T]),
            wrt=wrt, wit=wit, lam=lam, wg=wg, w1=w1, w2=w2, w3=w3,
            ws1=ws1, ws2=ws2, ws3=ws3,
            cident=ident, clfullt=lfullt, cones=ones, crepl=repl,
            ciota=iota, cmask=cm,
        ))
    return maps


_NC_CACHE = {}


def get_nc(debug=False):
    key = "dbg" if debug else "main"
    if key not in _NC_CACHE:
        nc = build(debug=debug)
        lower_extended_insts(nc)
        _NC_CACHE[key] = nc
    return _NC_CACHE[key]


def kernel(**inputs):
    nc = get_nc(debug=False)
    maps = make_inputs(inputs)
    res = run_bass_kernel_spmd(nc, maps, core_ids=list(range(NCORES)))
    outs = np.concatenate([np.asarray(r["out"]) for r in res.results], axis=0)
    return outs.reshape(2, 8192, D)
